# version 04
# baseline (speedup 1.0000x reference)
import numpy as np
import concourse.bass as bass
import concourse.mybir as mybir
from concourse.bass_utils import run_bass_kernel_spmd

F32 = mybir.dt.float32
F16 = mybir.dt.float16
I32 = mybir.dt.int32
I16 = mybir.dt.int16
AF = mybir.ActivationFunctionType
ALU = mybir.AluOpType

L = 2048; DM = 2048; DIN = 8384; DI = 4096; DXBC = 4224
NH = 64; HD = 64; NS = 64; CH = 128; NCH = L // CH
IFF = 5632; OGU = 2 * IFF; GS = 128

ENGS = ("sync", "scalar", "vector", "tensor", "gpsimd")
TAGS = []


class Stream:
    __slots__ = ("name", "count", "sem")

    def __init__(self, name):
        self.name = name
        self.count = 0
        self.sem = None


class PB:
    def __init__(self, nc):
        self.nc = nc
        self.items = {e: [] for e in ENGS}
        self.streams = []

    def new_stream(self, name):
        st = Stream(name)
        self.streams.append(st)
        return st

    def on(self, eng, fn, after=(), dma=False, small=False, stream=None):
        toks = []
        for t in after:
            if t is None:
                continue
            if t[0] == "S" and t[2] is None:
                t = ("S", t[1], t[1].count)
            toks.append(t)
        idx = len(self.items[eng])
        if dma:
            assert stream is not None
            stream.count += 16
            self.items[eng].append([fn, toks, True, False, small, stream, stream.count])
            return ("S", stream, stream.count)
        self.items[eng].append([fn, toks, False, False, small, None, 0])
        return ("E", eng, idx)

    def build(self, block, sems, stream_sems):
        for e in ENGS:
            for it in self.items[e]:
                for t in it[1]:
                    if t[0] == "E" and not self.items[t[1]][t[2]][2]:
                        self.items[t[1]][t[2]][3] = True
        semvals = {}
        for e in ENGS:
            c = 0
            vals = []
            for it in self.items[e]:
                if it[3]:
                    c += 1
                vals.append(c)
            semvals[e] = vals
        self.semvals = semvals
        for i, st in enumerate(self.streams):
            st.sem = stream_sems[i]

        def section(e):
            def body(eng):
                prev = {}
                for (fn, waits, is_dma, incn, small, stream, scount) in self.items[e]:
                    byk = {}
                    for t in waits:
                        if t[0] == "E":
                            pe, pidx = t[1], t[2]
                            p = self.items[pe][pidx]
                            if pe == e and not (p[4] or small):
                                continue
                            k = ("E", pe)
                            v = semvals[pe][pidx]
                        else:
                            k = ("S", id(t[1]))
                            v = t[2]
                            byk.setdefault("_obj", {})[id(t[1])] = t[1]
                        if k == "_obj":
                            continue
                        if v > byk.get(k, -1):
                            byk[k] = v
                    objs = byk.pop("_obj", {})
                    for k, v in sorted(byk.items(), key=lambda x: str(x[0])):
                        if prev.get(k, -1) >= v:
                            continue
                        prev[k] = v
                        if k[0] == "E":
                            eng.wait_ge(sems[k[1]], v)
                        else:
                            eng.wait_ge(objs[k[1]].sem, v)
                    inst = fn(eng)
                    if is_dma:
                        inst.then_inc(stream.sem, 16)
                    elif incn:
                        inst.then_inc(sems[e], 1)
            return body

        block.sync(section("sync"))
        block.scalar(section("scalar"))
        block.vector(section("vector"))
        block.tensor(section("tensor"))
        block.gpsimd(section("gpsimd"))

    def nops(self):
        return {e: len(self.items[e]) for e in ENGS}

    def tails(self):
        out = []
        for e in ENGS:
            for i in range(len(self.items[e]) - 1, -1, -1):
                if not self.items[e][i][2]:
                    out.append(("E", e, i))
                    break
        for st in self.streams:
            if st.count > 0:
                out.append(("S", st, st.count))
        return out


class Buf:
    __slots__ = ("ap", "w", "r", "stream", "group")

    def __init__(self, ap=None):
        self.ap = ap
        self.w = None
        self.r = []
        self.stream = None
        self.group = False

    def rd(self):
        if self.group and self.stream is not None and self.stream.count > 0:
            return [("S", self.stream, None)]
        return [self.w] if self.w else []

    def war(self):
        return self.r + ([self.w] if self.w else [])

    def setw(self, tok):
        self.w = tok
        self.r = []

    def setr(self, tok):
        self.r.append(tok)


class Pool:
    def __init__(self, nc, part_bytes):
        self.t = nc.alloc_sbuf_tensor("pool", [128, part_bytes // 2], F16)
        self.part_bytes = part_bytes

    def view(self, off, parts, n, dtype):
        esz = mybir.dt.size(dtype)
        a = self.t.ap()[0:parts, off // 2: off // 2 + (n * esz) // 2]
        if dtype != F16:
            a = a.bitcast(dtype)
        return a


class Alloc:
    def __init__(self, pool, start, fence=()):
        self.pool = pool
        self.p = start
        self.fence = list(fence)

    def take(self, parts, n, dtype, name=""):
        esz = mybir.dt.size(dtype)
        self.p = (self.p + 63) // 64 * 64
        ap = self.pool.view(self.p, parts, n, dtype)
        self.p += n * esz
        assert self.p <= self.pool.part_bytes, f"SBUF overflow at {name}: {self.p}"
        b = Buf(ap)
        b.r = list(self.fence)
        return b


class Builder:
    def __init__(self, dump=(), stop_after=None):
        self.dump = set(dump)
        self.stop_after = stop_after
        nc = bass.Bass("TRN2", target_bir_lowering=False)
        self.nc = nc
        self.pb = PB(nc)

        def dram(name, shape, dtype, kind="Internal"):
            k = "ExternalOutput" if name in self.dump else kind
            return Buf(nc.dram_tensor(name, shape, dtype, kind=k).ap())

        self.u = dram("u", [L, DM], F32, "ExternalInput")
        self.w_in = dram("w_in", [DIN, DM], F32, "ExternalInput")
        self.conv_diag = dram("conv_diag", [33, 4, 128, 128], F32, "ExternalInput")
        self.conv_b_t = dram("conv_b_t", [128, 33], F32, "ExternalInput")
        self.dt_bias = dram("dt_bias", [64, 1], F32, "ExternalInput")
        self.a_log = dram("a_log", [64, 1], F32, "ExternalInput")
        self.d_rep = dram("d_rep", [128, DI], F32, "ExternalInput")
        self.normw_t = dram("normw_t", [128, 32], F32, "ExternalInput")
        self.w_out = dram("w_out", [DM, DI], F32, "ExternalInput")
        self.gup = dram("gup", [OGU, DM // 8], I32, "ExternalInput")
        self.gus_lane = dram("gus_lane", [OGU, DM], F16, "ExternalInput")
        self.dnp = dram("dnp", [DM, IFF // 8], I32, "ExternalInput")
        self.dns_lane = dram("dns_lane", [DM, IFF], F16, "ExternalInput")
        self.ident32 = dram("ident32", [128, 128], F32, "ExternalInput")
        self.ident16 = dram("ident16", [128, 128], F16, "ExternalInput")
        self.ones32 = dram("ones32", [128, 128], F32, "ExternalInput")
        self.tri_incl = dram("tri_incl", [128, 128], F16, "ExternalInput")
        self.u_strict = dram("u_strict", [128, 128], F16, "ExternalInput")
        self.out = dram("out", [L, DM], F32, "ExternalOutput")
        self.w_in16 = dram("w_in16", [DIN, DM], F16)
        self.w_in16T = dram("w_in16T", [DM, DIN], F16)
        self.w_out16 = dram("w_out16", [DM, DI], F16)
        self.w_out16T = dram("w_out16T", [DI, DM], F16)
        self.gu16l = dram("gu16l", [OGU, DM], F16)
        self.gu16T = dram("gu16T", [DM, OGU], F16)
        self.dn16l = dram("dn16l", [DM, IFF], F16)
        self.dn16T = dram("dn16T", [IFF, DM], F16)
        self.ln1_tm = dram("ln1_tm", [L, DM], F16)
        self.zx_tm = dram("zx_tm", [L, DIN], F16)
        self.xbc16 = dram("xbc16", [DXBC, L], F16)
        self.adt16_d = dram("adt16_d", [64, L], F16)
        self.y_tm = dram("y_tm", [L, DI], F16)
        self.x2_tm = dram("x2_tm", [L, DM], F32)
        self.ln3_tm = dram("ln3_tm", [L, DM], F16)
        self.h_tm = dram("h_tm", [L, IFF], F16)

        self.pool = Pool(nc, 204800)
        al = Alloc(self.pool, 0)
        self.c_id32 = al.take(128, 128, F32, "id32")
        self.c_id16 = al.take(128, 128, F16, "id16")
        self.c_ones = al.take(128, 128, F32, "ones")
        self.c_tri = al.take(128, 128, F16, "tri")
        self.c_ust = al.take(128, 128, F16, "ust")
        self.c_convb = al.take(128, 33, F32, "convb")
        self.c_normw = al.take(128, 32, F32, "normw")
        self.c_dtb = al.take(64, 1, F32, "dtb")
        self.c_alog = al.take(64, 1, F32, "alog")
        self._cf = {}
        self._cf_bufs = []
        for _ in range(10):
            self._cf_bufs.append(al.take(128, 1, F32, "cf"))
        self.perm_end = al.p
        DQW = 2048
        dq = Alloc(self.pool, self.perm_end)
        self.dq_pk = [dq.take(128, 704, I32, "dqp") for _ in range(2)]
        self.dq_sc = [dq.take(128, DQW, F16, "dqs") for _ in range(2)]
        self.dq_t7 = dq.take(128, DQW, I16, "dqt7")
        self.dq_sb = dq.take(128, DQW, I16, "dqsb")
        self.dq_r1 = dq.take(128, DQW, F16, "dqr1")
        self.dq_r2 = dq.take(128, DQW, F16, "dqr2")
        self.dq_sg = dq.take(128, DQW, F16, "dqsg")
        self.dq_o = [dq.take(128, DQW, F16, "dqo") for _ in range(2)]
        self.dyn_start = dq.p
        self.ps = [Buf(nc.alloc_psum_tensor(f"ps{i}", [128, 512], F32).ap()) for i in range(8)]
        self.final_toks = []

    def group(self, bufs):
        st = self.pb.new_stream("g")
        for b in bufs:
            b.stream = st
            b.group = True
        return st

    def constf(self, val):
        if val not in self._cf:
            assert self._cf_bufs, "out of const slots"
            b = self._cf_bufs.pop()
            self.comp("vector", lambda e, o=b.ap, v=float(val): e.memset(o, v), writes=[b])
            self._cf[val] = b
        return self._cf[val]

    def dma(self, q, dst, dst_ap, srcs, src_ap, extra=(), transpose=False):
        after = list(extra)
        for s in srcs:
            after += s.rd()
        after += dst.war()
        if dst.stream is None:
            dst.stream = self.pb.new_stream("st")
        t = self.pb.on(q, (lambda e, o=dst_ap, i=src_ap, tr=transpose:
                           e.dma_start(out=o, in_=i, transpose=tr)), after, dma=True,
                       stream=dst.stream)
        dst.setw(t)
        for s in srcs:
            s.setr(t)
        return t

    def comp(self, eng, fn, reads=(), writes=(), extra=()):
        after = list(extra)
        for b in reads:
            after += b.rd()
        for b in writes:
            after += b.war()
        small = False
        for b in writes:
            try:
                if b.ap.free_nbytes() < 2048:
                    small = True
            except Exception:
                pass
        t = self.pb.on(eng, fn, after, small=small)
        for b in writes:
            b.setw(t)
        for b in reads:
            b.setr(t)
        return t

    def mm(self, outb, out_ap, lb, l_ap, rb, r_ap, start, stop, extra=()):
        return self.comp("tensor",
                         lambda e, o=out_ap, l1=l_ap, r1=r_ap, s1=start, s2=stop:
                         e.matmul(o, l1, r1, start=s1, stop=s2),
                         reads=[lb, rb], writes=[outb], extra=extra)

    def act(self, outb, out_ap, inb, in_ap, func, bias=0.0, scale=1.0, alpha=None,
            accum=None, reads=(), writes=(), extra=()):
        rl = list(reads)
        nparts = out_ap.shape[0]
        if isinstance(bias, float) and bias != 0.0:
            bb = self.constf(bias)
            bias = bb.ap[0:nparts, :]
            rl.append(bb)
        elif isinstance(bias, Buf):
            rl.append(bias)
            bias = bias.ap
            if bias.shape[0] != nparts:
                bias = bias[0:nparts, :]

        def fn(e, o=out_ap, i=in_ap, b=bias, s=scale, a=alpha, ac=accum):
            kw = dict(bias=b, scale=s)
            if a is not None:
                kw["alpha"] = a
            if ac is not None:
                kw["accum_out"] = ac
            return e.activation(o, i, func, **kw)
        return self.comp("scalar", fn, reads=[inb] + rl, writes=[outb] + list(writes),
                         extra=extra)

    def vcopy(self, dstb, dst_ap, srcb, src_ap, extra=()):
        return self.comp("vector", lambda e, o=dst_ap, i=src_ap: e.tensor_copy(o, i),
                         reads=[srcb], writes=[dstb], extra=extra)

    def tt(self, outb, o_ap, ab, a_ap, bb2, b_ap, op, extra=()):
        return self.comp("vector", lambda e, o=o_ap, a=a_ap, b=b_ap, p=op:
                         e.tensor_tensor(o, a, b, p),
                         reads=[ab, bb2], writes=[outb], extra=extra)

    def p0(self):
        self.group([self.c_id32, self.c_id16, self.c_ones, self.c_tri, self.c_ust,
                    self.c_convb, self.c_normw, self.c_dtb, self.c_alog])
        for (cb, src) in [(self.c_id32, self.ident32), (self.c_id16, self.ident16),
                          (self.c_ones, self.ones32), (self.c_tri, self.tri_incl),
                          (self.c_ust, self.u_strict), (self.c_convb, self.conv_b_t),
                          (self.c_normw, self.normw_t)]:
            self.dma("sync", cb, cb.ap, [src], src.ap)
        self.dma("sync", self.c_dtb, self.c_dtb.ap, [self.dt_bias], self.dt_bias.ap)
        self.dma("sync", self.c_alog, self.c_alog.ap, [self.a_log], self.a_log.ap)
        for i in range(8):
            sl = slice(i * (DIN // 8), (i + 1) * (DIN // 8))
            self.dma("gpsimd", self.w_in16, self.w_in16.ap[sl, :], [self.w_in],
                     self.w_in.ap[sl, :])
        for i in range(4):
            sl = slice(i * (DM // 4), (i + 1) * (DM // 4))
            self.dma("gpsimd", self.w_out16, self.w_out16.ap[sl, :], [self.w_out],
                     self.w_out.ap[sl, :])

    def p0_winT(self, al):
        slab = [al.take(128, DIN, F16, "tslab") for _ in range(2)]
        for dch in range(16):
            s = dch % 2
            self.dma("sync", slab[s], slab[s].ap, [self.w_in16],
                     self.w_in16.ap[0:DIN, dch * 128:(dch + 1) * 128], transpose=True)
            self.dma("gpsimd", self.w_in16T, self.w_in16T.ap[dch * 128:(dch + 1) * 128, :],
                     [slab[s]], slab[s].ap)

    def p0_woutT(self, al):
        slab = [al.take(128, DM, F16, "oslab") for _ in range(2)]
        for ich in range(32):
            s = ich % 2
            self.dma("sync", slab[s], slab[s].ap, [self.w_out16],
                     self.w_out16.ap[0:DM, ich * 128:(ich + 1) * 128], transpose=True)
            self.comp("vector", lambda e, a=slab[s].ap, nw=self.c_normw.ap[:, ich:ich + 1]:
                      e.tensor_scalar(a, a, nw, None, ALU.mult),
                      reads=[slab[s], self.c_normw], writes=[slab[s]])
            self.dma("gpsimd", self.w_out16T, self.w_out16T.ap[ich * 128:(ich + 1) * 128, :],
                     [slab[s]], slab[s].ap)

    def dq_chunk(self, which, oc, lanes=None):
        if which == "gu":
            packed, scl, dst, nin, lw = self.gup, self.gus_lane, self.gu16l, DM, DM // 4
        else:
            packed, scl, dst, nin, lw = self.dnp, self.dns_lane, self.dn16l, IFF, IFF // 4
        s = oc % 2
        rs = slice(oc * 128, (oc + 1) * 128)
        pk, sc = self.dq_pk[s], self.dq_sc[s]
        lanes = list(range(4)) if lanes is None else lanes
        w = lw * len(lanes)
        self.dma("sync", pk, pk.ap[:, 0:nin // 8], [packed], packed.ap[rs, :])
        c0 = lanes[0] * lw
        self.dma("sync", sc, sc.ap[:, 0:w], [scl], scl.ap[rs, c0:c0 + w])
        pk16 = pk.ap.bitcast(I16)[:, 0:nin // 4]
        t7, sb = self.dq_t7, self.dq_sb
        for i, j in enumerate(lanes):
            o = slice(i * lw, (i + 1) * lw)
            self.comp("vector", lambda e, o1=t7.ap[:, o], i1=pk16, sh=4 * j:
                      e.tensor_scalar(o1, i1, float(sh), 7.0, ALU.logical_shift_right,
                                      ALU.bitwise_and),
                      reads=[pk], writes=[t7])
            self.comp("vector", lambda e, o1=sb.ap[:, o], i1=pk16, sh=4 * j + 3:
                      e.tensor_scalar(o1, i1, float(sh), 1.0, ALU.logical_shift_right,
                                      ALU.bitwise_and),
                      reads=[pk], writes=[sb])
        r1, r2, sg, ot = self.dq_r1, self.dq_r2, self.dq_sg, self.dq_o[s]
        self.act(r1, r1.ap[:, 0:w], t7, t7.ap[:, 0:w], AF.Prelu, bias=-4.0, alpha=0.5)
        self.act(r2, r2.ap[:, 0:w], t7, t7.ap[:, 0:w], AF.Relu, bias=-6.0)
        self.comp("vector", lambda e, o=sg.ap[:, 0:w], i=sb.ap[:, 0:w]:
                  e.tensor_scalar(o, i, -2.0, 1.0, ALU.mult, ALU.add),
                  reads=[sb], writes=[sg])
        self.tt(sg, sg.ap[:, 0:w], sg, sg.ap[:, 0:w], sc, sc.ap[:, 0:w], ALU.mult)
        self.tt(r1, r1.ap[:, 0:w], r1, r1.ap[:, 0:w], r2, r2.ap[:, 0:w], ALU.add)
        self.comp("vector", lambda e, o=ot.ap[:, 0:w], a=r1.ap[:, 0:w], b=sg.ap[:, 0:w]:
                  e.scalar_tensor_tensor(o, a, 2.0, b, ALU.add, ALU.mult),
                  reads=[r1, sg], writes=[ot])
        self.dma("gpsimd", dst, dst.ap[rs, c0:c0 + w], [ot], ot.ap[:, 0:w])

    def dq_T(self, which, al):
        if which == "gu":
            src, dst, nrows, nin = self.gu16l, self.gu16T, OGU, DM
        else:
            src, dst, nrows, nin = self.dn16l, self.dn16T, DM, IFF
        lane = nin // 4
        half = nrows // 2
        slab = al.take(128, half, F16, "uslab")
        for c in range(nin // 128):
            jj = c // (lane // 128)
            m = c % (lane // 128)
            row0 = 4 * (128 * m) + jj
            for hv in range(2):
                self.dma("sync", slab, slab.ap, [src],
                         src.ap[hv * half:(hv + 1) * half, c * 128:(c + 1) * 128],
                         transpose=True)
                dst_ap = dst.ap.rearrange("(a b) n -> a b n", b=4)[
                    (row0 // 4):(row0 // 4) + 128, jj, hv * half:(hv + 1) * half]
                self.dma("gpsimd", dst, dst_ap, [slab], slab.ap)

    def ln_bufs(self, al, n):
        return (al.take(128, 2, F32, "susq"),
                al.take(128, 1, F32, "lnmu"),
                al.take(128, 1, F32, "lnr"))

    def ln_stats(self, bufs, srcb, src_ap, dumpb, dump_ap, n, eps):
        st2, mu, r = bufs
        self.act(dumpb, dump_ap, srcb, src_ap, AF.Identity, accum=st2.ap[:, 0:1],
                 writes=[st2])
        self.act(dumpb, dump_ap, srcb, src_ap, AF.Square, accum=st2.ap[:, 1:2],
                 writes=[st2])
        self.comp("vector", lambda e, o=mu.ap, a=st2.ap[:, 0:1], nn=float(n):
                  e.tensor_scalar(o, a, 1.0 / nn, None, ALU.mult),
                  reads=[st2], writes=[mu])
        self.comp("vector", lambda e, o=r.ap, a=mu.ap, b=mu.ap:
                  e.tensor_tensor(o, a, b, ALU.mult), reads=[mu], writes=[r])
        self.comp("vector", lambda e, o=r.ap, a=st2.ap[:, 1:2], s1=1.0 / float(n),
                  b=r.ap: e.scalar_tensor_tensor(o, a, s1, b, ALU.mult, ALU.subtract),
                  reads=[st2, r], writes=[r])
        self.act(r, r.ap, r, r.ap, AF.Ln, bias=eps)
        self.act(r, r.ap, r, r.ap, AF.Exp, scale=-0.5)
        return mu, r

    def ln_apply(self, bufs, srcb, src_ap, outb, out_ap, n, eps):
        st2, mu, r = bufs
        self.ln_stats(bufs, srcb, src_ap, outb, out_ap, n, eps)
        t = self.comp("vector", lambda e, o=out_ap, i=src_ap, m=mu.ap, rr=r.ap:
                      e.tensor_scalar(o, i, m, rr, ALU.subtract, ALU.mult),
                      reads=[srcb, mu, r], writes=[outb])
        return t

    def pA(self):
        al = Alloc(self.pool, self.dyn_start, fence=self.pb.tails())
        u2 = [al.take(128, DM, F32, "u2") for _ in range(2)]
        ln1o = [al.take(128, DM, F16, "ln1o") for _ in range(2)]
        lnb = self.ln_bufs(al, DM)
        for tc in range(16):
            s = tc % 2
            self.dma("sync", u2[s], u2[s].ap, [self.u], self.u.ap[tc * 128:(tc + 1) * 128, :])
            self.ln_apply(lnb, u2[s], u2[s].ap, ln1o[s], ln1o[s].ap, DM, 1e-5)
            self.dma("scalar", self.ln1_tm, self.ln1_tm.ap[tc * 128:(tc + 1) * 128, :],
                     [ln1o[s]], ln1o[s].ap)
        wt = [al.take(128, 16 * 512, F16, "wtile") for _ in range(2)]
        slabs2 = [[al.take(128, 1024, F16, f"sl{h}_{d}") for d in range(16)]
                  for h in range(2)]
        self.group(slabs2[0] + slabs2[1])
        zt = [al.take(128, 512, F16, "ztile") for _ in range(4)]
        dq_i = 0
        ocs = [512] * 16 + [192]
        for half in range(2):
            hr = slice(half * 1024, (half + 1) * 1024)
            for dch in range(16):
                self.dma("sync", slabs2[half][dch], slabs2[half][dch].ap, [self.ln1_tm],
                         self.ln1_tm.ap[hr, dch * 128:(dch + 1) * 128], transpose=True)
        for half in range(2):
            slabs = slabs2[half]
            for oc in range(17):
                w = ocs[oc]
                s = oc % 2
                src = self.w_in16T.ap.rearrange("(a p) n -> p a n", p=128)[
                    :, :, oc * 512: oc * 512 + w]
                wta = wt[s].ap.rearrange("p (a n) -> p a n", a=16)[:, :, 0:w]
                self.dma("sync", wt[s], wta, [self.w_in16T], src)
                for dch in range(16):
                    for t8 in range(8):
                        self.mm(self.ps[t8], self.ps[t8].ap[:, 0:w],
                                slabs[dch], slabs[dch].ap[:, t8 * 128:(t8 + 1) * 128],
                                wt[s], wta[:, dch, :],
                                start=(dch == 0), stop=(dch == 15))
                for t8 in range(8):
                    z = zt[t8 % 4]
                    self.vcopy(z, z.ap[:, 0:w], self.ps[t8], self.ps[t8].ap[:, 0:w])
                    self.dma("scalar", self.zx_tm,
                             self.zx_tm.ap[half * 1024 + t8 * 128: half * 1024 + (t8 + 1) * 128,
                                           oc * 512: oc * 512 + w],
                             [z], z.ap[:, 0:w])
                for _ in range(3):
                    if dq_i < 88:
                        self.dq_chunk("gu", dq_i)
                        dq_i += 1
        while dq_i < 88:
            self.dq_chunk("gu", dq_i)
            dq_i += 1

    def pB0(self):
        al = Alloc(self.pool, self.dyn_start, fence=self.pb.tails())
        xp = [al.take(128, 32 + L, F16, "xpad") for _ in range(2)]
        dg32 = [al.take(128, 512, F32, "dg32") for _ in range(2)]
        dg16 = [al.take(128, 512, F16, "dg16") for _ in range(2)]
        xo = [al.take(128, 512, F16, "xout") for _ in range(2)]
        for i in range(2):
            self.comp("vector", lambda e, o=xp[i].ap[:, 29:32]: e.memset(o, 0.0),
                      writes=[xp[i]])
        for fc in range(33):
            s = fc % 2
            self.dma("sync", xp[s], xp[s].ap[:, 32:32 + L], [self.zx_tm],
                     self.zx_tm.ap[0:L, DI + fc * 128: DI + (fc + 1) * 128], transpose=True)
            self.dma("sync", dg32[s], dg32[s].ap.rearrange("p (k b) -> p k b", k=4),
                     [self.conv_diag], self.conv_diag.ap[fc].transpose([1, 0, 2]))
            self.vcopy(dg16[s], dg16[s].ap, dg32[s], dg32[s].ap)
            for cc in range(4):
                bank = self.ps[4 + (cc % 2)]
                for k in range(4):
                    self.mm(bank, bank.ap, dg16[s], dg16[s].ap[:, k * 128:(k + 1) * 128],
                            xp[s], xp[s].ap[:, 29 + cc * 512 + k: 29 + cc * 512 + k + 512],
                            start=(k == 0), stop=(k == 3))
                x1 = xo[cc % 2]
                self.act(x1, x1.ap, bank, bank.ap, AF.Silu,
                         bias=Buf.__new__(Buf) if False else self.c_convb_col(fc))
                self.dma("scalar", self.xbc16,
                         self.xbc16.ap[fc * 128:(fc + 1) * 128, cc * 512:(cc + 1) * 512],
                         [x1], x1.ap)

    def c_convb_col(self, fc):
        b = Buf(self.c_convb.ap[:, fc:fc + 1])
        b.w = self.c_convb.w
        return b

    def pBdt(self, al):
        dtr128 = al.take(128, L, F16, "dtr128")
        dtr = al.take(64, L, F16, "dtr")
        tmpa = al.take(64, L, F32, "tmpa")
        expA = al.take(64, 1, F32, "expA")
        self.dma("sync", dtr128, dtr128.ap, [self.zx_tm],
                 self.zx_tm.ap[0:L, DIN - 128:DIN], transpose=True)
        self.dma("scalar", self.adt16_d, self.adt16_d.ap, [dtr128], dtr128.ap[64:128, :])
        self.dma("sync", dtr, dtr.ap, [self.adt16_d], self.adt16_d.ap)
        self.comp("vector", lambda e, o=tmpa.ap, i=dtr.ap, s1=self.c_dtb.ap[:, 0:1]:
                  e.tensor_scalar(o, i, s1, None, ALU.add),
                  reads=[dtr, self.c_dtb], writes=[tmpa])
        ab = al.take(64, L, F32, "ab")
        self.act(ab, ab.ap, tmpa, tmpa.ap, AF.Abs)
        self.act(ab, ab.ap, ab, ab.ap, AF.Exp, scale=-1.0)
        self.act(ab, ab.ap, ab, ab.ap, AF.Ln, bias=1.0)
        self.act(self.b_dtv, self.b_dtv.ap, tmpa, tmpa.ap, AF.Relu)
        self.tt(self.b_dtv, self.b_dtv.ap, self.b_dtv, self.b_dtv.ap, ab, ab.ap, ALU.add)
        self.act(expA, expA.ap, self.c_alog, self.c_alog.ap, AF.Exp)
        self.comp("vector", lambda e, o=self.b_adt.ap, i=self.b_dtv.ap, s1=expA.ap:
                  e.tensor_scalar(o, i, s1, -1.0, ALU.mult, ALU.mult),
                  reads=[self.b_dtv, expA], writes=[self.b_adt])

    def pB(self):
        al = Alloc(self.pool, self.dyn_start, fence=self.pb.tails())
        self.b_state = al.take(64, DI, F32, "state")
        self.b_state16 = al.take(64, DI, F16, "state16")
        self.b_dtv = al.take(64, L, F32, "dtv")
        self.b_adt = al.take(64, L, F32, "adt")
        self.b_drep16 = al.take(128, DI, F16, "drep16")
        dt_mark = al.p
        self.pBdt(al)
        al.p = dt_mark
        al.fence = self.pb.tails()
        self.dma("gpsimd", self.b_drep16, self.b_drep16.ap, [self.d_rep], self.d_rep.ap)
        self.comp("vector", lambda e, o=self.b_state.ap: e.memset(o, 0.0),
                  writes=[self.b_state])
        self.comp("vector", lambda e, o=self.b_state16.ap: e.memset(o, 0.0),
                  writes=[self.b_state16])
        xt = [al.take(128, DXBC, F16, "xt") for _ in range(2)]
        at = [al.take(128, 64, F16, "at") for _ in range(2)]
        bB = [al.take(64, 128, F16, "bB") for _ in range(2)]
        bC = [al.take(64, 128, F16, "bC") for _ in range(2)]
        ztile = al.take(128, DI, F16, "ztile")
        ytile = [al.take(128, DI, F16, "ytile") for _ in range(2)]
        acum = al.take(64, CH, F32, "acum")
        atmp = al.take(64, CH, F32, "atmp")
        acuT = al.take(128, 64, F32, "acuT")
        eaT = al.take(128, 64, F16, "eaT")
        dech = al.take(64, CH, F16, "dech")
        decT = al.take(128, 64, F16, "decT")
        dtvT = al.take(128, 64, F32, "dtvT")
        etot = al.take(64, 1, F32, "etot")
        erow = al.take(1, 64, F32, "erow")
        cbm = al.take(128, 128, F16, "cbm")
        cbrep = al.take(128, 2048, F16, "cbrep")
        Rb = al.take(128, NH * CH, F16, "R")
        Gt = Rb
        xd = al.take(128, DI, F16, "xd")
        xdd = al.take(128, DI, F16, "xdd")
        xD = al.take(128, DI, F16, "xD")
        sz = al.take(128, DI, F16, "sz")
        ssq = al.take(128, 1, F32, "ssq")
        rstd = al.take(128, 1, F32, "rstdB")
        self.pB_high = al.p
        dq_dn = 0
        for c in range(NCH):
            s = c % 2
            cs = slice(c * CH, (c + 1) * CH)
            self.dma("sync", xt[s], xt[s].ap, [self.xbc16], self.xbc16.ap[0:DXBC, cs],
                     transpose=True)
            pass
            self.dma("sync", bB[s], bB[s].ap, [self.xbc16], self.xbc16.ap[DI:DI + 64, cs])
            self.dma("sync", bC[s], bC[s].ap, [self.xbc16],
                     self.xbc16.ap[DI + 64:DI + 128, cs])
            self.dma("sync", ztile, ztile.ap, [self.zx_tm], self.zx_tm.ap[cs, 0:DI])
            self.comp("vector", lambda e, o=acum.ap, i=self.b_adt.ap[:, cs]:
                      e.tensor_tensor_scan(o, i, i, 0.0, ALU.add, ALU.bypass),
                      reads=[self.b_adt], writes=[acum])
            p7 = self.ps[7]
            self.vcopy(atmp, atmp.ap, self.b_adt, self.b_adt.ap[:, cs])
            self.mm(p7, p7.ap[:, 448:512], atmp, atmp.ap,
                    self.c_id32, self.c_id32.ap[0:64, 0:64], True, True)
            self.act(at[s], at[s].ap, p7, p7.ap[:, 448:512], AF.Identity)
            self.mm(p7, p7.ap[:, 0:64], acum, acum.ap, self.c_id32,
                    self.c_id32.ap[0:64, 0:64], True, True)
            self.vcopy(acuT, acuT.ap, p7, p7.ap[:, 0:64])
            self.act(eaT, eaT.ap, acuT, acuT.ap, AF.Exp)
            self.comp("vector", lambda e, o=dech.ap, i=acum.ap, s1=acum.ap[:, CH - 1:CH]:
                      e.tensor_scalar(o, i, s1, None, ALU.subtract),
                      reads=[acum], writes=[dech])
            self.act(dech, dech.ap, dech, dech.ap, AF.Exp, scale=-1.0)
            self.act(etot, etot.ap, acum, acum.ap[:, CH - 1:CH], AF.Exp, reads=[acum])
            self.mm(p7, p7.ap[:, 64:128], dech, dech.ap, self.c_id16,
                    self.c_id16.ap[0:64, 0:64], True, True)
            self.act(decT, decT.ap, p7, p7.ap[:, 64:128], AF.Identity)
            self.mm(p7, p7.ap[:, 128:192], self.b_dtv, self.b_dtv.ap[:, cs],
                    self.c_id32, self.c_id32.ap[0:64, 0:64], True, True)
            self.vcopy(dtvT, dtvT.ap, p7, p7.ap[:, 128:192])
            self.mm(p7, p7.ap[0:1, 192:256], etot, etot.ap, self.c_id32,
                    self.c_id32.ap[0:64, 0:64], True, True)
            self.vcopy(erow, erow.ap, p7, p7.ap[0:1, 192:256])
            self.mm(p7, p7.ap[0:64, 256:320], self.c_ones, self.c_ones.ap[0:1, 0:64],
                    erow, erow.ap, True, True)
            self.mm(p7, p7.ap[:, 320:448], bB[s], bB[s].ap, bC[s], bC[s].ap, True, True)
            self.tt(cbm, cbm.ap, p7, p7.ap[:, 320:448], self.c_tri, self.c_tri.ap, ALU.mult)
            self.comp("vector",
                      lambda e, o=cbrep.ap.rearrange("p (a b) -> p a b", a=16),
                      i=cbm.ap.unsqueeze(1).broadcast_to([128, 16, 128]):
                      e.tensor_copy(o, i), reads=[cbm], writes=[cbrep])
            self.comp("vector",
                      lambda e, o=Rb.ap.rearrange("p (a b) -> p a b", a=64),
                      a=at[s].ap.unsqueeze(2).broadcast_to([128, 64, 128]),
                      b=self.c_tri.ap.unsqueeze(1).broadcast_to([128, 64, 128]):
                      e.tensor_tensor(o, a, b, ALU.mult),
                      reads=[at[s], self.c_tri], writes=[Rb])
            for g2 in range(16):
                bank = self.ps[5 + (g2 % 2)]
                self.mm(bank, bank.ap, self.c_ust, self.c_ust.ap,
                        Rb, Rb.ap[:, g2 * 512:(g2 + 1) * 512], True, True)
                self.act(Gt, Gt.ap[:, g2 * 512:(g2 + 1) * 512], bank, bank.ap, AF.Exp)
            for q in range(4):
                self.tt(Gt, Gt.ap[:, q * 2048:(q + 1) * 2048],
                        Gt, Gt.ap[:, q * 2048:(q + 1) * 2048], cbrep, cbrep.ap, ALU.mult)
            self.comp("vector",
                      lambda e, o=xd.ap.rearrange("p (a b) -> p a b", a=64),
                      a=xt[s].ap[:, 0:DI].rearrange("p (a b) -> p a b", a=64),
                      b=dtvT.ap.unsqueeze(2).broadcast_to([128, 64, 64]):
                      e.tensor_tensor(o, a, b, ALU.mult),
                      reads=[xt[s], dtvT], writes=[xd])
            self.comp("vector",
                      lambda e, o=xdd.ap.rearrange("p (a b) -> p a b", a=64),
                      a=xd.ap.rearrange("p (a b) -> p a b", a=64),
                      b=decT.ap.unsqueeze(2).broadcast_to([128, 64, 64]):
                      e.tensor_tensor(o, a, b, ALU.mult),
                      reads=[xd, decT], writes=[xdd])
            self.tt(xD, xD.ap, xt[s], xt[s].ap[:, 0:DI], self.b_drep16,
                    self.b_drep16.ap, ALU.mult)
            yt = ytile[s]
            for gq in range(8):
                colr = slice(gq * 512, (gq + 1) * 512)
                bA = self.ps[0 + (gq % 2)]
                bBk = self.ps[2 + (gq % 2)]
                self.mm(bA, bA.ap, bC[s], bC[s].ap, self.b_state16,
                        self.b_state16.ap[:, colr], True, True)
                for hh in range(8):
                    h = gq * 8 + hh
                    self.mm(bBk, bBk.ap[:, hh * 64:(hh + 1) * 64],
                            Gt, Gt.ap[:, h * 128:(h + 1) * 128],
                            xd, xd.ap[:, h * 64:(h + 1) * 64],
                            start=(hh == 0), stop=False)
                self.mm(bBk, bBk.ap, self.c_id16, self.c_id16.ap, xD, xD.ap[:, colr],
                        False, True)
                self.comp("vector",
                          lambda e, o=yt.ap[:, colr].rearrange("p (a b) -> p a b", a=8),
                          a=bA.ap.rearrange("p (a b) -> p a b", a=8),
                          b=eaT.ap[:, gq * 8:(gq + 1) * 8].unsqueeze(2).broadcast_to([128, 8, 64]):
                          e.tensor_tensor(o, a, b, ALU.mult),
                          reads=[bA, eaT], writes=[yt])
                self.tt(yt, yt.ap[:, colr], yt, yt.ap[:, colr], bBk, bBk.ap, ALU.add)
            self.comp("vector",
                      lambda e, o=self.b_state.ap.rearrange("p (a b) -> p a b", a=64),
                      a=self.b_state.ap.rearrange("p (a b) -> p a b", a=64),
                      b=p7.ap[0:64, 256:320].unsqueeze(2).broadcast_to([64, 64, 64]):
                      e.tensor_tensor(o, a, b, ALU.mult),
                      reads=[self.b_state, p7], writes=[self.b_state])
            for gq in range(8):
                colr = slice(gq * 512, (gq + 1) * 512)
                b4 = self.ps[4]
                self.mm(b4, b4.ap[0:64, :], xt[s], xt[s].ap[:, DI:DI + 64],
                        xdd, xdd.ap[:, colr], True, True)
                self.tt(self.b_state, self.b_state.ap[:, colr],
                        self.b_state, self.b_state.ap[:, colr], b4, b4.ap[0:64, :], ALU.add)
            self.vcopy(self.b_state16, self.b_state16.ap, self.b_state, self.b_state.ap)
            self.act(sz, sz.ap, ztile, ztile.ap, AF.Silu)
            self.tt(yt, yt.ap, yt, yt.ap, sz, sz.ap, ALU.mult)
            self.act(sz, sz.ap, yt, yt.ap, AF.Square, accum=ssq.ap, writes=[ssq])
            self.act(rstd, rstd.ap, ssq, ssq.ap, AF.Ln, bias=1e-5, scale=1.0 / DI)
            self.act(rstd, rstd.ap, rstd, rstd.ap, AF.Exp, scale=-0.5)
            self.comp("vector", lambda e, o=yt.ap, i=yt.ap, s1=rstd.ap:
                      e.tensor_scalar(o, i, s1, None, ALU.mult),
                      reads=[yt, rstd], writes=[yt])
            self.dma("scalar", self.y_tm, self.y_tm.ap[cs, :], [yt], yt.ap)
            if dq_dn < 64:
                self.dq_chunk("dn", dq_dn // 4, lanes=[dq_dn % 4])
                self.dq_chunk("dn", (dq_dn + 1) // 4, lanes=[(dq_dn + 1) % 4])
                self.dq_chunk("dn", (dq_dn + 2) // 4, lanes=[(dq_dn + 2) % 4])
                self.dq_chunk("dn", (dq_dn + 3) // 4, lanes=[(dq_dn + 3) % 4])
                dq_dn += 4

    def pC(self):
        al = Alloc(self.pool, self.perm_end, fence=self.pb.tails())
        slabs = [al.take(128, 512, F16, f"ysl{i}") for i in range(32)]
        self.group(slabs)
        wt = [al.take(128, 16 * 512, F16, "cwt") for _ in range(2)]
        attn = [al.take(128, DM, F32, f"attn{i}") for i in range(4)]
        u2 = [al.take(128, DM, F32, "cu") for _ in range(2)]
        x2t = [al.take(128, DM, F32, "x2t") for _ in range(2)]
        ln3o = [al.take(128, DM, F16, "ln3o") for _ in range(2)]
        st2 = al.take(128, 2, F32, "cst2")
        rstd = al.take(128, 1, F32, "crstd")
        mr = al.take(128, 1, F32, "cmr")
        adump = al.take(128, DM, F16, "adump")
        lnb3 = self.ln_bufs(al, DM)
        guT_al = Alloc(self.pool, al.p, fence=self.pb.tails())
        self.dq_T("gu", guT_al)
        for qt in range(4):
            qr = slice(qt * 512, (qt + 1) * 512)
            for ich in range(32):
                self.dma("sync", slabs[ich], slabs[ich].ap, [self.y_tm],
                         self.y_tm.ap[qr, ich * 128:(ich + 1) * 128], transpose=True)
            for oc in range(4):
                for kh in range(2):
                    s = kh
                    src = self.w_out16T.ap.rearrange("(a p) n -> p a n", p=128)[
                        :, kh * 16:(kh + 1) * 16, oc * 512:(oc + 1) * 512]
                    wta = wt[s].ap.rearrange("p (a n) -> p a n", a=16)
                    self.dma("sync", wt[s], wta, [self.w_out16T], src)
                    for i16 in range(16):
                        ich = kh * 16 + i16
                        for t4 in range(4):
                            self.mm(self.ps[t4], self.ps[t4].ap,
                                    slabs[ich], slabs[ich].ap[:, t4 * 128:(t4 + 1) * 128],
                                    wt[s], wta[:, i16, :],
                                    start=(ich == 0), stop=(ich == 31))
                for t4 in range(4):
                    self.vcopy(attn[t4], attn[t4].ap[:, oc * 512:(oc + 1) * 512],
                               self.ps[t4], self.ps[t4].ap)
            for t4 in range(4):
                tc = qt * 4 + t4
                s = t4 % 2
                rows = slice(tc * 128, (tc + 1) * 128)
                self.dma("sync", u2[s], u2[s].ap, [self.u], self.u.ap[rows, :])
                self.act(adump, adump.ap, attn[t4], attn[t4].ap, AF.Identity,
                         accum=st2.ap[:, 0:1], writes=[st2])
                self.act(adump, adump.ap, attn[t4], attn[t4].ap, AF.Square,
                         accum=st2.ap[:, 1:2], writes=[st2])
                self.comp("vector", lambda e, o=mr.ap, a=st2.ap[:, 0:1]:
                          e.tensor_scalar(o, a, 1.0 / float(DM), None, ALU.mult),
                          reads=[st2], writes=[mr])
                self.comp("vector", lambda e, o=rstd.ap, a=mr.ap, b=mr.ap:
                          e.tensor_tensor(o, a, b, ALU.mult), reads=[mr], writes=[rstd])
                self.comp("vector", lambda e, o=rstd.ap, a=st2.ap[:, 1:2],
                          s1=1.0 / float(DM), b=rstd.ap:
                          e.scalar_tensor_tensor(o, a, s1, b, ALU.mult, ALU.subtract),
                          reads=[st2, rstd], writes=[rstd])
                self.act(rstd, rstd.ap, rstd, rstd.ap, AF.Ln, bias=1e-5)
                self.act(rstd, rstd.ap, rstd, rstd.ap, AF.Exp, scale=-0.5)
                self.comp("vector", lambda e, o=mr.ap, a=mr.ap, b=rstd.ap:
                          e.scalar_tensor_tensor(o, a, -1.0, b, ALU.mult, ALU.mult),
                          reads=[mr, rstd], writes=[mr])
                self.comp("vector", lambda e, o=x2t[s].ap, i=u2[s].ap, s1=mr.ap:
                          e.tensor_scalar(o, i, s1, None, ALU.add),
                          reads=[u2[s], mr], writes=[x2t[s]])
                self.comp("vector", lambda e, o=x2t[s].ap, a=attn[t4].ap, s1=rstd.ap,
                          b=x2t[s].ap: e.scalar_tensor_tensor(o, a, s1, b, ALU.mult, ALU.add),
                          reads=[attn[t4], rstd, x2t[s]], writes=[x2t[s]])
                self.dma("scalar", self.x2_tm, self.x2_tm.ap[rows, :], [x2t[s]], x2t[s].ap)
                self.ln_apply(lnb3, x2t[s], x2t[s].ap, ln3o[s], ln3o[s].ap, DM, 1e-5)
                self.dma("scalar", self.ln3_tm, self.ln3_tm.ap[rows, :], [ln3o[s]], ln3o[s].ap)

    def pD1(self):
        al = Alloc(self.pool, self.perm_end, fence=self.pb.tails())
        slabs = [al.take(128, 1024, F16, f"d1sl{i}") for i in range(16)]
        self.group(slabs)
        wt = [al.take(128, 16 * 512, F16, "d1wt") for _ in range(2)]
        g16 = [al.take(128, 512, F16, f"g16_{i}") for i in range(8)]
        h16 = [al.take(128, 512, F16, "h16") for _ in range(2)]
        dnT_al = Alloc(self.pool, al.p)
        self.dq_T("dn", dnT_al)
        for half in range(2):
            hr = slice(half * 1024, (half + 1) * 1024)
            for dch in range(16):
                self.dma("sync", slabs[dch], slabs[dch].ap, [self.ln3_tm],
                         self.ln3_tm.ap[hr, dch * 128:(dch + 1) * 128], transpose=True)
            for op in range(11):
                for gi, oc in ((0, op), (1, op + 11)):
                    s = gi
                    src = self.gu16T.ap.rearrange("(a p) n -> p a n", p=128)[
                        :, :, oc * 512:(oc + 1) * 512]
                    wta = wt[s].ap.rearrange("p (a n) -> p a n", a=16)
                    self.dma("sync", wt[s], wta, [self.gu16T], src)
                    for dch in range(16):
                        for t8 in range(8):
                            self.mm(self.ps[t8], self.ps[t8].ap,
                                    slabs[dch], slabs[dch].ap[:, t8 * 128:(t8 + 1) * 128],
                                    wt[s], wta[:, dch, :],
                                    start=(dch == 0), stop=(dch == 15))
                    if gi == 0:
                        for t8 in range(8):
                            self.act(g16[t8], g16[t8].ap, self.ps[t8], self.ps[t8].ap,
                                     AF.Silu)
                    else:
                        for t8 in range(8):
                            hh = h16[t8 % 2]
                            self.comp("vector",
                                      lambda e, o=hh.ap, a=g16[t8].ap, b=self.ps[t8].ap:
                                      e.scalar_tensor_tensor(o, a, 1.0 / 16.0, b,
                                                             ALU.mult, ALU.mult),
                                      reads=[g16[t8], self.ps[t8]], writes=[hh])
                            self.dma("scalar", self.h_tm,
                                     self.h_tm.ap[half * 1024 + t8 * 128:
                                                  half * 1024 + (t8 + 1) * 128,
                                                  op * 512:(op + 1) * 512],
                                     [hh], hh.ap)

    def pD2(self):
        al = Alloc(self.pool, self.perm_end, fence=self.pb.tails())
        slabs = [al.take(128, 1024, F16, f"d2sl{i}") for i in range(44)]
        self.group(slabs)
        wt = [al.take(128, 16 * 512, F16, "d2wt") for _ in range(2)]
        ffn = [al.take(128, DM, F16, f"ffn{i}") for i in range(8)]
        stats = [al.take(128, 8, F32, f"dst{i}") for i in range(8)]
        x2t = [al.take(128, DM, F32, "d2x2") for _ in range(2)]
        outt = [al.take(128, DM, F32, "outt") for _ in range(2)]
        sqd = al.take(128, 512, F16, "sqdump")
        mu = al.take(128, 1, F32, "dmu")
        ms = al.take(128, 1, F32, "dms")
        Rr = al.take(128, 1, F32, "dR")
        mr = al.take(128, 1, F32, "dmr")
        kg = [(0, 16), (16, 32), (32, 44)]
        for half in range(2):
            hr = slice(half * 1024, (half + 1) * 1024)
            for ich in range(44):
                self.dma("sync", slabs[ich], slabs[ich].ap, [self.h_tm],
                         self.h_tm.ap[hr, ich * 128:(ich + 1) * 128], transpose=True)
            for oc in range(4):
                for (k0, k1) in kg:
                    s = (k0 // 16) % 2
                    nk = k1 - k0
                    src = self.dn16T.ap.rearrange("(a p) n -> p a n", p=128)[
                        :, k0:k1, oc * 512:(oc + 1) * 512]
                    wta = wt[s].ap.rearrange("p (a n) -> p a n", a=16)[:, 0:nk, :]
                    self.dma("sync", wt[s], wta, [self.dn16T], src)
                    for ik in range(nk):
                        ich = k0 + ik
                        for t8 in range(8):
                            self.mm(self.ps[t8], self.ps[t8].ap,
                                    slabs[ich], slabs[ich].ap[:, t8 * 128:(t8 + 1) * 128],
                                    wt[s], wta[:, ik, :],
                                    start=(ich == 0), stop=(ich == 43))
                for t8 in range(8):
                    self.act(ffn[t8], ffn[t8].ap[:, oc * 512:(oc + 1) * 512],
                             self.ps[t8], self.ps[t8].ap, AF.Identity, scale=1.0 / 64,
                             accum=stats[t8].ap[:, oc:oc + 1], writes=[stats[t8]])
                    self.act(sqd, sqd.ap, self.ps[t8], self.ps[t8].ap, AF.Square,
                             scale=1.0 / 64,
                             accum=stats[t8].ap[:, 4 + oc:5 + oc], writes=[stats[t8]])
            for t8 in range(8):
                tc = half * 8 + t8
                rows = slice(tc * 128, (tc + 1) * 128)
                s = t8 % 2
                self.dma("sync", x2t[s], x2t[s].ap, [self.x2_tm], self.x2_tm.ap[rows, :])
                self.comp("vector", lambda e, o=mu.ap, i=stats[t8].ap[:, 0:4]:
                          e.tensor_reduce(o, i, mybir.AxisListType.X, ALU.add),
                          reads=[stats[t8]], writes=[mu])
                self.comp("vector", lambda e, o=ms.ap, i=stats[t8].ap[:, 4:8]:
                          e.tensor_reduce(o, i, mybir.AxisListType.X, ALU.add),
                          reads=[stats[t8]], writes=[ms])
                self.comp("vector", lambda e, o=mu.ap, i=mu.ap:
                          e.tensor_scalar(o, i, 1.0 / DM, None, ALU.mult),
                          reads=[mu], writes=[mu])
                self.comp("vector", lambda e, o=ms.ap, i=ms.ap:
                          e.tensor_scalar(o, i, 1.0 / DM, None, ALU.mult),
                          reads=[ms], writes=[ms])
                self.comp("vector", lambda e, o=Rr.ap, a=mu.ap, b=mu.ap:
                          e.tensor_tensor(o, a, b, ALU.mult), reads=[mu], writes=[Rr])
                self.comp("vector", lambda e, o=Rr.ap, a=ms.ap, b=Rr.ap:
                          e.scalar_tensor_tensor(o, a, (1.0 + 1e-5), b, ALU.mult, ALU.subtract),
                          reads=[ms, Rr], writes=[Rr])
                self.act(Rr, Rr.ap, Rr, Rr.ap, AF.Ln, bias=1e-5 * 1e-6 / (1024.0 * 1024.0))
                self.act(Rr, Rr.ap, Rr, Rr.ap, AF.Exp, scale=-0.5)
                self.comp("vector", lambda e, o=mr.ap, a=mu.ap, b=Rr.ap:
                          e.scalar_tensor_tensor(o, a, -1.0, b, ALU.mult, ALU.mult),
                          reads=[mu, Rr], writes=[mr])
                self.comp("vector", lambda e, o=outt[s].ap, i=x2t[s].ap, s1=mr.ap:
                          e.tensor_scalar(o, i, s1, None, ALU.add),
                          reads=[x2t[s], mr], writes=[outt[s]])
                self.comp("vector", lambda e, o=outt[s].ap, a=ffn[t8].ap, s1=Rr.ap,
                          b=outt[s].ap: e.scalar_tensor_tensor(o, a, s1, b, ALU.mult, ALU.add),
                          reads=[ffn[t8], Rr, outt[s]], writes=[outt[s]])
                t = self.dma("scalar", self.out, self.out.ap[rows, :], [outt[s]], outt[s].ap)
                self.final_toks.append(t)

    def finish(self):
        self.comp("vector", lambda e: e.engine_nop(), extra=self.final_toks)

    def build_block(self):
        nc = self.nc
        sems = {}
        import contextlib
        self._stack = contextlib.ExitStack()
        for e in ENGS:
            sems[e] = self._stack.enter_context(nc.semaphore(f"sem_{e}"))
        ssems = [self._stack.enter_context(nc.semaphore(f"ss{i}"))
                 for i in range(len(self.pb.streams))]
        block = self._stack.enter_context(nc.Block())
        self.pb.build(block, sems, ssems)
        self._stack.close()
        return nc


def build_full(dump=(), stop_after=None):
    b = Builder(dump=dump, stop_after=stop_after)
    b.p0()
    al0 = Alloc(b.pool, b.dyn_start + 100 * 1024)
    b.p0_winT(al0)
    phases = [("A", b.pA),
              ("B0", b.pB0),
              ("WOT", lambda: b.p0_woutT(Alloc(b.pool, b.dyn_start + 20 * 1024,
                                               fence=b.pb.tails()))),
              ("B", b.pB), ("C", b.pC), ("D1", b.pD1), ("D2", b.pD2)]
    for name, fn in phases:
        fn()
        if stop_after == name:
            break
    b.finish()
    b.build_block()
    return b


def host_consts(inp):
    conv_w = np.asarray(inp["L0_attn_conv_w"], np.float32)
    conv_b = np.asarray(inp["L0_attn_conv_b"], np.float32)
    cd = np.zeros((33, 4, 128, 128), np.float32)
    r = np.arange(128)
    cw = conv_w.reshape(33, 128, 4)
    for k in range(4):
        cd[:, k, r, r] = cw[:, :, k]
    gus = np.asarray(inp["L0_ffn_gate_up_scales"])
    dns = np.asarray(inp["L0_ffn_down_scales"])
    gse = np.repeat(gus.T, GS, axis=1)
    gsl = gse.reshape(OGU, DM // 4, 4).transpose(0, 2, 1).reshape(OGU, DM).copy()
    dse = np.repeat(dns.T, GS, axis=1)
    dsl = dse.reshape(DM, IFF // 4, 4).transpose(0, 2, 1).reshape(DM, IFF).copy()
    jj = np.arange(128)
    consts = {
        "w_in": np.ascontiguousarray(inp["L0_attn_W_in"], dtype=np.float32),
        "conv_diag": cd,
        "conv_b_t": np.ascontiguousarray(conv_b.reshape(33, 128).T),
        "dt_bias": np.ascontiguousarray(np.asarray(inp["L0_attn_dt_bias"], np.float32).reshape(64, 1)),
        "a_log": np.ascontiguousarray(np.asarray(inp["L0_attn_A_log"], np.float32).reshape(64, 1)),
        "d_rep": np.ascontiguousarray(np.broadcast_to(
            np.repeat(np.asarray(inp["L0_attn_D_param"], np.float32), HD)[None, :], (128, DI))),
        "normw_t": np.ascontiguousarray(np.asarray(inp["L0_attn_norm_w"], np.float32).reshape(32, 128).T),
        "w_out": np.ascontiguousarray(inp["L0_attn_W_out"], dtype=np.float32),
        "gup": np.ascontiguousarray(inp["L0_ffn_gate_up_packed"], dtype=np.int32),
        "gus_lane": np.ascontiguousarray(gsl),
        "dnp": np.ascontiguousarray(inp["L0_ffn_down_packed"], dtype=np.int32),
        "dns_lane": np.ascontiguousarray(dsl),
        "ident32": np.eye(128, dtype=np.float32),
        "ident16": np.eye(128, dtype=np.float16),
        "ones32": np.ones((128, 128), np.float32),
        "tri_incl": np.ascontiguousarray((jj[:, None] <= jj[None, :]).astype(np.float16)),
        "u_strict": np.ascontiguousarray((jj[:, None] > jj[None, :]).astype(np.float16)),
    }
    return consts


def kernel(**inputs):
    consts = host_consts(inputs)
    u = np.asarray(inputs["L0_attn_u"], np.float32)
    b = build_full()
    in_maps = []
    for core in range(8):
        m = dict(consts)
        m["u"] = np.ascontiguousarray(u[core])
        in_maps.append(m)
    res = run_bass_kernel_spmd(b.nc, in_maps, core_ids=list(range(8)))
    return np.stack([r["out"] for r in res.results], axis=0)
